# version 21
# baseline (speedup 1.0000x reference)
import numpy as np
import ml_dtypes
import contextlib

import concourse.bass as bass
import concourse.mybir as mybir
from concourse.bass_utils import run_bass_kernel_spmd
from concourse.alu_op_type import AluOpType
from concourse.aot_env import allow_ici_all_to_all

F32 = mybir.dt.float32
BF16 = mybir.dt.bfloat16
AF = mybir.ActivationFunctionType
OP = AluOpType
bf16 = ml_dtypes.bfloat16

NCORE = 8
T = 8192
D = 2048
HD = 128
NH = 2
CH = 512
NCH = T // CH
INTER = 5504
NIT = INTER // 128
NDM = D // 128
TOK = 2 * CH
NTS = TOK // 128
EPS = 1e-5
OSCALE = float(D) ** -0.5
C_SHIFT = 0.0

ENGS = ("pe", "act", "dve", "gp", "sp")


class Builder:
    def __init__(self, nc):
        self.nc = nc
        self.ops = {e: [] for e in ENGS}
        self.sems = {}
        self.counts = {}
        self.es = contextlib.ExitStack()

    def sem(self, name):
        if name not in self.sems:
            self.sems[name] = self.es.enter_context(self.nc.semaphore(name))
            self.counts[name] = 0
        return self.sems[name]

    def val(self, name):
        return self.counts.get(name, 0)

    def emit(self, eng, fn, waits=(), inc=(), dma=False):
        step = 16 if dma else 1
        out_vals = {}
        incs = []
        for s in inc:
            self.sem(s)
            self.counts[s] += step
            out_vals[s] = self.counts[s]
            incs.append((self.sems[s], step))
        merged = {}
        for (s, v) in waits:
            if v > 0:
                merged[s] = max(merged.get(s, 0), v)
        wl = [(self.sem(s), v) for s, v in merged.items()]
        self.ops[eng].append((wl, fn, incs))
        return out_vals

    def build_block(self):
        ops = self.ops
        with self.nc.Block() as block:
            def runner(eng_name):
                def _run(eng):
                    for wl, fn, incs in ops[eng_name]:
                        for sem, v in wl:
                            eng.wait_ge(sem, v)
                        instr = fn(eng)
                        for sem, step in incs:
                            if instr is None:
                                instr = eng.nop()
                            instr = instr.then_inc(sem, step)
                return _run
            block.tensor(runner("pe"))
            block.scalar(runner("act"))
            block.vector(runner("dve"))
            block.gpsimd(runner("gp"))
            block.sync(runner("sp"))


class Arena:

    def __init__(self, tensor, ncols, align=16):
        self.t = tensor
        self.n = ncols
        self.align = align
        self.live = {}

    def alloc(self, name, cols):
        assert name not in self.live
        al = self.align
        taken = sorted(self.live.values())
        start = 0
        for s, ln in taken:
            if start + cols <= s:
                break
            start = -(-(s + ln) // al) * al
        assert start + cols <= self.n, (
            f"arena overflow: {name} needs {cols} @ {start}, cap {self.n}")
        self.live[name] = (start, cols)
        return start

    def free(self, *names):
        for nm in names:
            del self.live[nm]

    def ap(self, name, c0=0, cols=None, p0=0, pn=128):
        s, ln = self.live[name]
        if cols is None:
            cols = ln - c0
        assert c0 + cols <= ln, (name, c0, cols, ln)
        return self.t[p0:p0 + pn, s + c0:s + c0 + cols]


def build_nc():
    nc = bass.Bass("TRN2", target_bir_lowering=False)

    def din(name, shape, dt=F32):
        return nc.dram_tensor(name, shape, dt, kind="ExternalInput").ap()

    xT = din("xT", [D, T])
    x_own = din("x_own", [TOK, D])
    wq = din("wq", [D, NH * HD])
    wk = din("wk", [D, HD])
    wv = din("wv", [D, HD])
    wo = din("wo", [D, D])
    bq = din("bq", [128, NH])
    bk = din("bk", [HD, 1])
    bv = din("bv", [HD, 1])
    bo = din("bo", [1, D])
    cosd = din("cosd", [128, 2 * T])
    sind = din("sind", [128, 2 * T])
    maskd = din("maskd", [128, 4 * CH], BF16)
    lutd = din("lutd", [128, 16])
    gidx = din("gidx", [D // NCORE, INTER], BF16)
    uidx = din("uidx", [D // NCORE, INTER], BF16)
    didx = din("didx", [INTER // NCORE, D], BF16)
    out_d = nc.dram_tensor("out", [TOK, D], F32, kind="ExternalOutput").ap()

    ag_g_in = nc.dram_tensor("ag_g_in", [D // NCORE, INTER], BF16).ap()
    ag_u_in = nc.dram_tensor("ag_u_in", [D // NCORE, INTER], BF16).ap()
    ag_d_in = nc.dram_tensor("ag_d_in", [INTER // NCORE, D], BF16).ap()
    ag_g = nc.dram_tensor("ag_g", [D, INTER], BF16, addr_space="Shared").ap()
    ag_u = nc.dram_tensor("ag_u", [D, INTER], BF16, addr_space="Shared").ap()
    ag_d = nc.dram_tensor("ag_d", [INTER, D], BF16, addr_space="Shared").ap()
    a2a_in = [nc.dram_tensor(f"a2a_in{i}", [NCORE * NH * HD, CH], BF16).ap()
              for i in range(2)]
    a2a_out = [nc.dram_tensor(f"a2a_out{i}", [NCORE * NH * HD, CH], BF16).ap()
               for i in range(2)]
    wo_bf = nc.dram_tensor("wo_bf", [D, D], BF16).ap()

    B = Builder(nc)
    es = B.es

    BF_COLS = 80400
    F32_COLS = 13000
    abf = Arena(es.enter_context(nc.sbuf_tensor("abf", [128, BF_COLS], BF16)),
                BF_COLS, align=16)
    af = Arena(es.enter_context(nc.sbuf_tensor("af32", [128, F32_COLS], F32)),
               F32_COLS, align=8)
    pb = [es.enter_context(nc.psum_tensor(f"pb{i}", [128, 512], F32))
          for i in range(8)]

    af.alloc("lut", 16)
    af.alloc("bq", NH)
    af.alloc("bk", 1)
    af.alloc("bv", 1)
    af.alloc("bo_bc", D)
    af.alloc("sonesf", 128)
    af.alloc("ln", 16)
    af.alloc("epsv", 1)
    af.alloc("bn6", 24)
    abf.alloc("masks", 4 * CH)
    abf.alloc("sones", 1)

    def d_sp(sem, dst, src, waits=()):
        v = B.emit("sp", lambda e, dst=dst, src=src: e.dma_start(out=dst, in_=src),
                   waits=waits, inc=(sem,), dma=True)
        return (sem, v[sem])

    def d_gp(sem, dst, src, waits=()):
        v = B.emit("gp", lambda e, dst=dst, src=src: e.dma_start(out=dst, in_=src),
                   waits=waits, inc=(sem,), dma=True)
        return (sem, v[sem])

    def tsp(sem, dst, src, waits=()):
        v = B.emit("sp", lambda e, dst=dst, src=src: e.dma_start(
            out=dst, in_=src, transpose=True), waits=waits, inc=(sem,), dma=True)
        return (sem, v[sem])

    def tgp(sem, dst, src, waits=()):
        v = B.emit("act", lambda e, dst=dst, src=src: e.dma_start(
            out=dst, in_=src, transpose=True), waits=waits, inc=(sem,), dma=True)
        return (sem, v[sem])

    def mm(out, lhsT, rhs, start, stop, waits=(), inc=()):
        return B.emit("pe", lambda e, out=out, lhsT=lhsT, rhs=rhs, start=start,
                      stop=stop: e.matmul(out, lhsT, rhs, start=start, stop=stop),
                      waits=waits, inc=inc)

    def act(outp, inp, func, bias, scale, waits=()):
        return B.emit("act", lambda e, outp=outp, inp=inp, func=func, bias=bias,
                      scale=scale: e.activation(outp, inp, func, bias=bias,
                                                scale=scale),
                      waits=waits, inc=("act",))

    def tt(outp, in0, in1, op, waits=()):
        return B.emit("dve", lambda e, outp=outp, in0=in0, in1=in1, op=op:
                      e.tensor_tensor(out=outp, in0=in0, in1=in1, op=op),
                      waits=waits, inc=("dve",))

    def dve_chain():
        return ("dve", B.val("dve"))

    d_sp("s_ld", af.ap("lut"), lutd)
    d_sp("s_ld", af.ap("bq"), bq)
    d_sp("s_ld", af.ap("bk"), bk)
    d_sp("s_ld", af.ap("bv"), bv)
    d_sp("s_ld", abf.ap("masks"), maskd)
    ld0 = d_sp("s_ld", af.ap("bo_bc", 0, D, pn=1), bo)

    abf.alloc("xbuf", 2 * NDM * CH)
    abf.alloc("wq", NDM * NH * HD)
    abf.alloc("wk", NDM * HD)
    abf.alloc("wv", NDM * HD)
    af.alloc("csa", 2 * CH)
    af.alloc("csb", 2 * CH)
    af.alloc("tmpA", 2 * 2 * CH)
    af.alloc("tmpB", 2 * 2 * CH)
    af.alloc("ru", 2 * 2 * CH)
    abf.alloc("vbfT", 2 * CH)
    abf.alloc("kT", T)
    abf.alloc("qT", NH * T)
    abf.alloc("v", T)
    abf.alloc("p", 4 * CH)
    abf.alloc("o", 2 * CH)
    af.alloc("lrow", CH)
    af.alloc("lrec", CH)
    af.alloc("bc", 2 * CH)

    d_gp("s_ldw", abf.ap("wq"), wq.rearrange("(n p) m -> p n m", p=128))
    d_gp("s_ldw", abf.ap("wk"), wk.rearrange("(n p) m -> p n m", p=128))
    ldw = d_gp("s_ldw", abf.ap("wv"), wv.rearrange("(n p) m -> p n m", p=128))
    B.emit("dve", lambda e, a=abf.ap("sones"): e.memset(a, 1.0), inc=("dve",))
    B.emit("dve", lambda e, a=af.ap("epsv"): e.memset(a, EPS), inc=("dve",))
    ones_init = B.emit("dve", lambda e, a=af.ap("sonesf", pn=1): e.memset(a, OSCALE),
                       inc=("dve",))["dve"]

    def wtiles(nrows):
        tl, r = [], 0
        while r < nrows:
            tl.append((r, min(128, nrows - r)))
            r += 128
        return tl

    abf.alloc("idx", 2752)
    abf.alloc("wacc", 2752)
    abf.alloc("wcon", 2752)
    wb_done = {}
    gather_tiles = []
    for mat, idx_d, ag_in_d, fd in (("g", gidx, ag_g_in, INTER),
                                    ("u", uidx, ag_u_in, INTER),
                                    ("d", didx, ag_d_in, D)):
        cw = 2752 if fd == INTER else fd
        tl = []
        r = 0
        while r < idx_d.shape[0]:
            nr = min(128, idx_d.shape[0] - r)
            for c0 in range(0, fd, cw):
                tl.append((mat, idx_d, ag_in_d, r, nr, c0, min(cw, fd - c0)))
            r += 128
        for i, t in enumerate(tl):
            gather_tiles.append(t + (i == len(tl) - 1,))

    _gstate = {"prev_store": None, "emitted": 0}
    gather_q = []

    def _mk_quarters():
        for (mat, idx_d, ag_in_d, r0, nr, c0, cwid, last) in gather_tiles:
            idx_ap = abf.ap("idx", 0, cwid, pn=nr)
            acc_ap = abf.ap("wacc", 0, cwid, pn=nr)
            con_ap = abf.ap("wcon", 0, cwid, pn=nr)

            def jpass(j, idx_ap=idx_ap, acc_ap=acc_ap, con_ap=con_ap, nr=nr):
                if j == 0:
                    B.emit("dve", lambda e, o=acc_ap, i=idx_ap,
                           s2=af.ap("lut", 0, 1, pn=nr):
                           e.tensor_scalar(out=o, in0=i, scalar1=0.0,
                                           scalar2=s2, op0=OP.is_equal,
                                           op1=OP.mult),
                           waits=[ld0, _gstate["idx_lv"], dve_chain()],
                           inc=("dve",))
                else:
                    B.emit("dve", lambda e, o=con_ap, i=idx_ap, j=j,
                           s2=af.ap("lut", j, 1, pn=nr):
                           e.tensor_scalar(out=o, in0=i, scalar1=float(j),
                                           scalar2=s2, op0=OP.is_equal,
                                           op1=OP.mult),
                           waits=[dve_chain()], inc=("dve",))
                    tt(acc_ap, acc_ap, con_ap, OP.add, waits=[dve_chain()])

            def q0(idx_ap=idx_ap, nr=nr, idx_d=idx_d, r0=r0, c0=c0,
                   cwid=cwid, jpass=jpass):
                ps = _gstate["prev_store"]
                lv = d_gp("s_idx", idx_ap, idx_d[r0:r0 + nr, c0:c0 + cwid],
                          waits=[ps] if ps else [])
                _gstate["idx_lv"] = lv
                for j in range(0, 4):
                    jpass(j)
            gather_q.append(q0)
            gather_q.append(lambda jp=jpass: [jp(j) for j in range(4, 8)])
            gather_q.append(lambda jp=jpass: [jp(j) for j in range(8, 12)])

            def q3(acc_ap=acc_ap, nr=nr, ag_in_d=ag_in_d, r0=r0, c0=c0,
                   cwid=cwid, mat=mat, last=last, jpass=jpass):
                for j in range(12, 16):
                    jpass(j)
                _gstate["prev_store"] = d_gp(
                    "s_wst_" + mat, ag_in_d[r0:r0 + nr, c0:c0 + cwid],
                    acc_ap, waits=[dve_chain()])
                if last:
                    wb_done[mat] = _gstate["prev_store"]
            gather_q.append(q3)
    _mk_quarters()
    NGQ = len(gather_q)

    def pace_gather(num, den):
        want = min(NGQ, NGQ * num // den)
        while _gstate["emitted"] < want:
            gather_q[_gstate["emitted"]]()
            _gstate["emitted"] += 1

    x_dma = {}
    pe_xfree = {}
    act_evac = {}
    kq_done = {}
    vT_done = {}
    vt_chunk = {}
    bank_users = {}
    b4 = {"act": 0, "dve": 0}
    b5 = {"dve": 0}

    def emit_p_chunk(tc):
        par = tc % 2
        w = [("pe", pe_xfree[tc - 2])] if tc >= 2 else []
        x_dma[tc] = d_gp("s_x%d" % par, abf.ap("xbuf", par * NDM * CH, NDM * CH),
                         xT[:, tc * CH:(tc + 1) * CH].rearrange(
                             "(n p) m -> p n m", p=128), waits=w)
        csw = [("dve", kq_done[tc - 1])] if tc >= 1 else []
        d_sp("s_cs0", af.ap("csa"),
             cosd[:, tc * 2 * CH:(tc + 1) * 2 * CH], waits=csw)
        cs_val = d_sp("s_cs0", af.ap("csb"),
                      sind[:, tc * 2 * CH:(tc + 1) * 2 * CH])

        roped_prev = kq_done.get(tc - 2, 0)
        tA = lambda c0, cols, p0=0, pn=128: af.ap("tmpA", par * 2 * CH + c0,
                                                  cols, p0=p0, pn=pn)
        tB = lambda c0, cols: af.ap("tmpB", par * 2 * CH + c0, cols, pn=64)
        pew = [x_dma[tc], ldw]
        for b in (par, 2 + par):
            if b in bank_users:
                pew.append(bank_users[b])
        for dm in range(NDM):
            iv = mm(pb[par][:, :], abf.ap("wk", dm * HD, HD),
                    abf.ap("xbuf", par * NDM * CH + dm * CH, CH),
                    start=(dm == 0), stop=(dm == NDM - 1),
                    waits=pew if dm == 0 else (),
                    inc=("pe",) if dm == NDM - 1 else ())
        pe_k = iv["pe"]
        for dm in range(NDM):
            iv = mm(pb[2 + par][:, :], abf.ap("wv", dm * HD, HD),
                    abf.ap("xbuf", par * NDM * CH + dm * CH, CH),
                    start=(dm == 0), stop=(dm == NDM - 1),
                    inc=("pe",) if dm == NDM - 1 else ())
        pe_v = iv["pe"]
        pe_q = {}
        for dm in range(NDM):
            qw = [("act", b4["act"]), ("dve", b4["dve"])] if dm == 0 else ()
            iv = mm(pb[4][:, :], abf.ap("wq", dm * NH * HD + 0 * HD, HD),
                    abf.ap("xbuf", par * NDM * CH + dm * CH, CH),
                    start=(dm == 0), stop=(dm == NDM - 1), waits=qw,
                    inc=("pe",) if dm == NDM - 1 else ())
        pe_q[0] = iv["pe"]
        act(tA(0, CH, 64, 64), pb[4][0:64, :], AF.Identity,
            af.ap("bq", 0, 1, p0=0, pn=64), 1.0,
            waits=[("pe", pe_q[0]), ("dve", roped_prev), ld0])
        q0v = act(tA(CH, CH, 64, 64), pb[4][64:128, :], AF.Identity,
                  af.ap("bq", 0, 1, p0=64, pn=64), 1.0)["act"]
        b4["act"] = q0v
        for dm in range(NDM):
            qw = [("act", b4["act"]), ("dve", b4["dve"])] if dm == 0 else ()
            iv = mm(pb[4][:, :], abf.ap("wq", dm * NH * HD + 1 * HD, HD),
                    abf.ap("xbuf", par * NDM * CH + dm * CH, CH),
                    start=(dm == 0), stop=(dm == NDM - 1), waits=qw,
                    inc=("pe",) if dm == NDM - 1 else ())
        pe_q[1] = iv["pe"]
        pe_xfree[tc] = pe_q[1]

        act(tA(0, CH, 0, 64), pb[par][0:64, :], AF.Identity,
            af.ap("bk", p0=0, pn=64), 1.0,
            waits=[("pe", pe_k), ("dve", roped_prev), ld0])
        kv = act(tA(CH, CH, 0, 64), pb[par][64:128, :], AF.Identity,
                 af.ap("bk", p0=64, pn=64), 1.0)["act"]
        bank_users[par] = ("act", kv)
        act(tB(0, CH), pb[4][0:64, :], AF.Identity,
            af.ap("bq", 1, 1, p0=0, pn=64), 1.0,
            waits=[("pe", pe_q[1])])
        evac_last = act(tB(CH, CH), pb[4][64:128, :], AF.Identity,
                        af.ap("bq", 1, 1, p0=64, pn=64), 1.0)["act"]
        b4["act"] = evac_last
        act_evac[("k", tc)] = evac_last
        act_evac[("q1", tc)] = evac_last
        vw = [("pe", pe_v)]
        if tc >= 2:
            vw.append(vT_done[tc - 2])
        act_evac[("v", tc)] = act(abf.ap("vbfT", par * CH, CH), pb[2 + par][:, :],
                                  AF.Identity, af.ap("bv"), 1.0, waits=vw)["act"]
        bank_users[2 + par] = ("act", act_evac[("v", tc)])

        for j in range(CH // 128):
            vts = tsp("s_vt%d" % par, abf.ap("v", tc * CH + j * 128, 128),
                      abf.ap("vbfT", par * CH + j * 128, 128),
                      waits=[("act", act_evac[("v", tc)])] if j == 0 else ())
        vT_done[tc] = vts
        vt_chunk[tc] = [("s_vt0", B.val("s_vt0")), ("s_vt1", B.val("s_vt1"))]

        u1 = lambda p0=0, pn=128, c0=0, cols=2 * CH: af.ap("ru", c0, cols,
                                                           p0=p0, pn=pn)
        u2 = lambda p0=0, pn=128, c0=0, cols=2 * CH: af.ap("ru", 2 * CH + c0,
                                                           cols, p0=p0, pn=pn)
        csa_t = lambda pn=128: af.ap("csa", 0, 2 * CH, pn=pn)
        csb_t = lambda pn=128: af.ap("csb", 0, 2 * CH, pn=pn)
        tt(u1(), tA(0, 2 * CH), csa_t(), OP.mult,
           waits=[("act", evac_last), cs_val, dve_chain()])
        tt(u2(), tA(0, 2 * CH), csb_t(), OP.mult, waits=[dve_chain()])
        tt(abf.ap("kT", tc * CH, CH, p0=0, pn=64),
           u1(0, 64, 0, CH), u1(0, 64, CH, CH), OP.subtract,
           waits=[dve_chain()])
        tt(abf.ap("kT", tc * CH, CH, p0=64, pn=64),
           u2(0, 64, 0, CH), u2(0, 64, CH, CH), OP.add, waits=[dve_chain()])
        tt(abf.ap("qT", 0 * T + tc * CH, CH, p0=0, pn=64),
           u1(64, 64, 0, CH), u1(64, 64, CH, CH), OP.subtract,
           waits=[dve_chain()])
        tt(abf.ap("qT", 0 * T + tc * CH, CH, p0=64, pn=64),
           u2(64, 64, 0, CH), u2(64, 64, CH, CH), OP.add, waits=[dve_chain()])
        tt(u1(0, 64), tB(0, 2 * CH), csa_t(64), OP.mult, waits=[dve_chain()])
        tt(u2(0, 64), tB(0, 2 * CH), csb_t(64), OP.mult, waits=[dve_chain()])
        tt(abf.ap("qT", 1 * T + tc * CH, CH, p0=0, pn=64),
           u1(0, 64, 0, CH), u1(0, 64, CH, CH), OP.subtract,
           waits=[dve_chain()])
        tt(abf.ap("qT", 1 * T + tc * CH, CH, p0=64, pn=64),
           u2(0, 64, 0, CH), u2(0, 64, CH, CH), OP.add, waits=[dve_chain()])
        kq_done[tc] = B.val("dve")

    NKT = CH // 128
    o_done = {}
    pv_free = {}
    o_store_seq = {}
    slot_free = {}
    pending_norm = [None]
    _norm = {"bv_prev": 0, "bc_prev": 0, "l_free": 0}
    _a = {"gg": 0, "seq": 0}

    def emit_norm(hp, seq_, qc_, h_, lv_):
        lc = B.emit("dve", lambda e, o=af.ap("lrow", 0, CH, pn=1),
                    i=pb[5][0:1, :]: e.tensor_copy(o, i),
                    waits=[("pe", lv_), dve_chain()], inc=("dve",))["dve"]
        b5["dve"] = lc
        lf = B.emit("dve", lambda e, o=af.ap("lrec", 0, CH, pn=1),
                    i=af.ap("lrow", 0, CH, pn=1): e.reciprocal(o, i),
                    waits=[dve_chain(), ("pe", _norm["bv_prev"])],
                    inc=("dve",))["dve"]
        _norm["l_free"] = lf
        bw = [("dve", max(lf, b4["dve"])), ("act", b4["act"])]
        bv_ = mm(pb[4][:, :], af.ap("sonesf", 0, 128, pn=1),
                 af.ap("lrec", 0, CH, pn=1), True, True, waits=bw,
                 inc=("pe",))["pe"]
        _norm["bv_prev"] = bv_
        cb = B.emit("dve", lambda e, o=af.ap("bc", hp * CH, CH),
                    i=pb[4][:, :]: e.tensor_copy(o, i),
                    waits=[("pe", bv_), dve_chain()], inc=("dve",))["dve"]
        _norm["bc_prev"] = cb
        b4["dve"] = cb
        ow = [("dve", cb)]
        if seq_ >= 2:
            ow.append(o_store_seq[seq_ - 2])
        om = tt(abf.ap("o", hp * CH, CH), pb[6 + hp][:, :],
                af.ap("bc", hp * CH, CH), OP.mult, waits=ow)["dve"]
        pv_free[hp] = om
        half = qc_ // (NCH // 2)
        dest = qc_ % (NCH // 2)
        ov = d_sp("s_o%d%d" % (half, hp),
                  a2a_in[half][dest * NH * HD + h_ * HD:
                               dest * NH * HD + (h_ + 1) * HD, :],
                  abf.ap("o", hp * CH, CH), waits=[("dve", om)])
        o_done[(h_, qc_)] = ov
        o_store_seq[seq_] = ov

    def flush_norm():
        if pending_norm[0] is not None:
            pending_norm[0]()
            pending_norm[0] = None
            pace_gather(_a["seq"], 28)

    def emit_attn_pair(qc):
        pair_ready = [("dve", max(kq_done[qc], ones_init)), ld0] + vt_chunk[qc]
        for h in range(NH):
            seq = _a["seq"]
            hp = seq % 2
            ng2 = 2 * (qc + 1)
            nkt = 4 * (qc + 1)
            pend_avl = None
            gg0 = None
            lv = None
            for g in range(ng2 + 1):
                if g < ng2:
                    gg = _a["gg"]
                    bp = gg % 2
                    w = list(pair_ready) if g == 0 else []
                    for b in (2 * bp, 2 * bp + 1):
                        if b in bank_users:
                            w.append(bank_users[b])
                    for j in range(2):
                        kt = 2 * g + j
                        qv = mm(pb[2 * bp + j][:, :],
                                abf.ap("kT", kt * 128, 128),
                                abf.ap("qT", h * T + qc * CH, CH), True, True,
                                waits=w if j == 0 else (),
                                inc=("pe",) if j == 1 else ())
                    qk_g = qv["pe"]
                    if g == 0:
                        flush_norm()
                    ew = [("pe", qk_g)]
                    if bp in slot_free:
                        ew.append(("pe", slot_free[bp]))
                    for j in range(2):
                        avv = B.emit(
                            "act", lambda e, o=abf.ap("p", (2 * bp + j) * CH, CH),
                            i=pb[2 * bp + j][:, :]:
                            e.activation(o, i, AF.Exp, bias=-C_SHIFT, scale=1.0),
                            waits=ew if j == 0 else (), inc=("act",))["act"]
                    bank_users[2 * bp] = ("act", avv)
                    bank_users[2 * bp + 1] = ("act", avv)
                    pdep = ("act", avv)
                    if 2 * g >= nkt - 4:
                        for j in range(2):
                            kt = 2 * g + j
                            mj = kt - (nkt - 4)
                            mv = tt(abf.ap("p", (2 * bp + j) * CH, CH),
                                    abf.ap("p", (2 * bp + j) * CH, CH),
                                    abf.ap("masks", mj * CH, CH), OP.mult,
                                    waits=[("act", avv), dve_chain()] if j == 0
                                    else [dve_chain()])["dve"]
                        pdep = ("dve", mv)
                if pend_avl is not None:
                    pg, ppdep = pend_avl
                    pbp = pg % 2
                    for j in range(2):
                        kt = 2 * (pg - gg0) + j
                        avw = []
                        if j == 0:
                            avw.append(ppdep)
                            if kt == 0 and hp in pv_free:
                                avw.append(("dve", pv_free[hp]))
                        mm(pb[6 + hp][:, :], abf.ap("v", kt * 128, 128),
                           abf.ap("p", (2 * pbp + j) * CH, CH),
                           start=(kt == 0), stop=(kt == nkt - 1), waits=avw)
                        lw = []
                        if kt == 0:
                            lw.append(("dve", max(_norm["l_free"], b5["dve"])))
                        lvv = mm(pb[5][0:1, :], abf.ap("sones"),
                                 abf.ap("p", (2 * pbp + j) * CH, CH),
                                 start=(kt == 0), stop=(kt == nkt - 1),
                                 waits=lw, inc=("pe",) if j == 1 else ())
                        if j == 1:
                            lv = lvv["pe"]
                    slot_free[pbp] = lv
                if g < ng2:
                    if g == 0:
                        gg0 = _a["gg"]
                    pend_avl = (_a["gg"], pdep)
                    _a["gg"] += 1
            pending_norm[0] = (lambda hp=hp, s=seq, qc_=qc, h_=h, lv_=lv:
                               emit_norm(hp, s, qc_, h_, lv_))
            _a["seq"] += 1

    for tc in range(NCH):
        flush_norm()
        emit_p_chunk(tc)
        if tc >= 1:
            emit_attn_pair(tc - 1)
    emit_attn_pair(NCH - 1)
    flush_norm()
    pace_gather(1, 1)

    proj_done_dve = max(kq_done.values())
    vt_final = [("s_vt0", B.val("s_vt0")), ("s_vt1", B.val("s_vt1"))]
    abf.free("xbuf", "wq", "wk", "wv", "vbfT")
    af.free("csa", "csb", "tmpA", "tmpB", "ru")

    bo_w = [("pe", pe_xfree[NCH - 1]), ("act", b4["act"]),
            ("dve", max(ones_init, b4["dve"])), ld0]
    bo_bc_done = 0
    for c4 in range(4):
        pv = mm(pb[4][:, :], af.ap("sonesf", 0, 128, pn=1),
                af.ap("bo_bc", c4 * 512, 512, pn=1), True, True,
                waits=bo_w if c4 == 0 else [("dve", bo_bc_done)],
                inc=("pe",))["pe"]
        bo_bc_done = B.emit("dve", lambda e, o=af.ap("bo_bc", c4 * 512, 512),
                            i=pb[4][:, :]:
                            e.tensor_scalar(out=o, in0=i, scalar1=float(D) ** 0.5,
                                            scalar2=None, op0=OP.mult,
                                            op1=OP.bypass),
                            waits=[("pe", pv), dve_chain()], inc=("dve",))["dve"]

    cc_vals = {}

    def collective(kind, ins, outs, waits):
        def fn(e, kind=kind, ins=ins, outs=outs):
            if kind == "AllToAll":
                with allow_ici_all_to_all():
                    return e.collective_compute(
                        kind, OP.bypass, replica_groups=[list(range(NCORE))],
                        ins=ins, outs=outs)
            return e.collective_compute(
                kind, OP.bypass, replica_groups=[list(range(NCORE))],
                ins=ins, outs=outs)
        return B.emit("gp", fn, waits=waits, inc=("cc",))["cc"]

    cc_vals["a1"] = collective(
        "AllToAll", [a2a_in[0].opt()], [a2a_out[0].opt()],
        [("s_o00", B.val("s_o00")), ("s_o01", B.val("s_o01"))])
    cc_vals["a2"] = collective(
        "AllToAll", [a2a_in[1].opt()], [a2a_out[1].opt()],
        [("s_o10", B.val("s_o10")), ("s_o11", B.val("s_o11"))])
    cc_vals["g"] = collective("AllGather", [ag_g_in.opt()], [ag_g.opt()],
                              [wb_done["g"]])
    cc_vals["u"] = collective("AllGather", [ag_u_in.opt()], [ag_u.opt()],
                              [wb_done["u"]])
    cc_vals["d"] = collective("AllGather", [ag_d_in.opt()], [ag_d.opt()],
                              [wb_done["d"]])
    for q4 in range(4):
        wo_cast = d_gp("s_wocast", wo_bf[q4 * 512:(q4 + 1) * 512, :],
                       wo[q4 * 512:(q4 + 1) * 512, :])

    abf.free("kT", "qT", "v", "p", "o", "idx", "wacc", "wcon")
    af.free("lrow", "lrec", "bc")

    abf.alloc("oown", NDM * TOK)
    abf.alloc("woslab", 2 * D)
    abf.alloc("x1bf", NTS * D)
    abf.alloc("x1T", NDM * TOK)
    af.alloc("xbo", 2 * D)
    af.alloc("r1", 2 * D)

    gather_flushed = [wb_done["d"], ("s_idx", B.val("s_idx"))]
    for dt in range(NDM):
        for half in range(2):
            d_sp("s_ol", abf.ap("oown", dt * TOK + half * CH, CH),
                 a2a_out[half][dt * 128:(dt + 1) * 128, :],
                 waits=[("cc", cc_vals["a1" if half == 0 else "a2"])]
                 + (gather_flushed if (dt == 0 and half == 0) else []))
    oload_all = ("s_ol", B.val("s_ol"))

    NOROUND = NTS // 2
    wo_pe = {}
    evac_done = {}
    x1_done = {}
    x1T_done = {}
    xbo_add = {}

    for rnd in range(NOROUND):
        ts0, ts1 = 2 * rnd, 2 * rnd + 1
        for ts in (ts0, ts1):
            pr = ts % 2
            w = []
            if ts >= 2:
                w.append(("dve", evac_done[(rnd - 1, (ts % 2) * 4 + 3)]))
            xv = d_sp("s_xo%d" % pr, af.ap("xbo", pr * D, D),
                      x_own[ts * 128:(ts + 1) * 128, :], waits=w)
            xbo_add[ts] = tt(af.ap("xbo", pr * D, D), af.ap("xbo", pr * D, D),
                             af.ap("bo_bc"), OP.add,
                             waits=[xv, ("dve", max(B.val("dve"),
                                                    bo_bc_done))])["dve"]
        for dt in range(NDM):
            par = dt % 2
            w = []
            key_prev = (rnd, dt - 2) if dt >= 2 else (rnd - 1, dt + NDM - 2)
            if key_prev in wo_pe:
                w.append(("pe", wo_pe[key_prev]))
            ww = w + ([wo_cast] if (rnd == 0 and dt < 2) else [])
            gval = d_gp("s_wo%d" % par, abf.ap("woslab", par * D, D),
                        wo_bf[dt * 128:(dt + 1) * 128, :], waits=ww)
            for i, ts in enumerate((ts0, ts1)):
                mw = [gval]
                if dt == 0:
                    mw.append(oload_all)
                    mw.append(("dve", bo_bc_done))
                    for c4 in range(4):
                        if (rnd - 1, i * 4 + c4) in evac_done:
                            mw.append(("dve", evac_done[(rnd - 1, i * 4 + c4)]))
                for c4 in range(4):
                    iv = mm(pb[i * 4 + c4][:, :],
                            abf.ap("oown", dt * TOK + ts * 128, 128),
                            abf.ap("woslab", par * D + c4 * 512, 512),
                            start=(dt == 0), stop=(dt == NDM - 1),
                            waits=mw if c4 == 0 else (),
                            inc=("pe",) if (c4 == 3 and i == 1) else ())
            wo_pe[(rnd, dt)] = iv["pe"]
        mm_done = wo_pe[(rnd, NDM - 1)]
        for i, ts in enumerate((ts0, ts1)):
            pr = ts % 2
            for c4 in range(4):
                ev = tt(af.ap("r1", pr * D + c4 * 512, 512), pb[i * 4 + c4][:, :],
                        af.ap("xbo", pr * D + c4 * 512, 512), OP.add,
                        waits=[("pe", mm_done),
                               ("act", x1_done.get(ts - 2, 0)),
                               ("dve", max(B.val("dve"), xbo_add[ts]))])["dve"]
                evac_done[(rnd, i * 4 + c4)] = ev
            dvw = B.val("dve")
            for c4 in range(4):
                B.emit("dve", lambda e, o=af.ap("bn6", c4 * 6, 6),
                       i=af.ap("r1", pr * D + c4 * 512, 512): e.bn_stats(o, i),
                       waits=[("dve", dvw)] if c4 == 0 else (), inc=("dve",))
            B.emit("dve", lambda e, o=af.ap("ln", 0, 2), i=af.ap("bn6"):
                   e.bn_aggr(o, i), waits=[dve_chain()], inc=("dve",))
            sq = act(af.ap("ln", 2, 1), af.ap("ln", 1, 1), AF.Sqrt, af.ap("epsv"), 1.0,
                     waits=[dve_chain()])["act"]
            B.emit("dve", lambda e, o=af.ap("ln", 3, 1), i=af.ap("ln", 2, 1):
                   e.reciprocal(o, i), waits=[("act", sq), dve_chain()],
                   inc=("dve",))
            nm = B.emit("dve", lambda e, o=af.ap("ln", 4, 1), i=af.ap("ln", 0, 1),
                        s1=af.ap("ln", 3, 1):
                        e.tensor_scalar(out=o, in0=i, scalar1=s1, scalar2=-1.0,
                                        op0=OP.mult, op1=OP.mult),
                        waits=[dve_chain()], inc=("dve",))["dve"]
            x1_done[ts] = act(abf.ap("x1bf", ts * D, D), af.ap("r1", pr * D, D),
                              AF.Identity, af.ap("ln", 4, 1), af.ap("ln", 3, 1),
                              waits=[("dve", nm)])["act"]
            sem_t = "s_x1t%d" % (ts // 4)
            for dt in range(NDM):
                tv = tsp(sem_t, abf.ap("x1T", dt * TOK + ts * 128, 128),
                         abf.ap("x1bf", ts * D + dt * 128, 128),
                         waits=[("act", x1_done[ts])] if dt == 0 else ())
            x1T_done[ts] = [(sem_t, B.val(sem_t))]
    x1T_all = [("s_x1t0", B.val("s_x1t0")), ("s_x1t1", B.val("s_x1t1"))]
    x1T_th = {0: x1T_done[3], 1: x1T_done[7]}
    oproj_pe_done = wo_pe[(NOROUND - 1, NDM - 1)]

    abf.free("oown", "woslab")
    af.free("xbo", "r1")
    abf.alloc("hT", NIT * CH)
    abf.alloc("wgslab", 2 * 2 * D)
    abf.alloc("wuslab", 2 * 2 * D)
    abf.alloc("wdslab", 2 * D)
    af.free("bo_bc", "sonesf")
    af.alloc("sg", 2 * 512)
    af.alloc("r2", 4 * D)
    af.alloc("outb", D)

    out_stores = []
    r2_guard = {0: 0, 1: 0}
    gslab_pe = {}
    prev_down_pe = 0
    th_barrier_dve = 0
    th_barrier_act = 0
    for th in range(2):
        silu_done = {}
        hmul_done = {}
        slab_vals = {}
        for it in range(NIT):
            par = it % 2
            itp = it // 2
            ppar = itp % 2
            if it % 2 == 0:
                nits = min(2, NIT - it)
                w = []
                if (th, it - 4) in gslab_pe:
                    w.append(("pe", gslab_pe[(th, it - 3)]))
                elif th == 1 and it < 4 and (th - 1, NIT - 4 + it) in gslab_pe:
                    w.append(("pe", gslab_pe[(th - 1, NIT - 3 + it)]))
                if th == 1 and it < 4:
                    w.append(("pe", prev_down_pe))
                ccw = [("cc", cc_vals["g"])] if (th == 0 and it < 4) else []
                d_sp("s_wgu%d" % ppar, abf.ap("wgslab", ppar * 2 * D, nits * D),
                     ag_g[:, it * 128:(it + nits) * 128].rearrange(
                         "(n p) m -> p n m", p=128), waits=w + ccw)
                ccw = [("cc", cc_vals["u"])] if (th == 0 and it < 4) else []
                slab_vals[itp] = d_sp(
                    "s_wgu%d" % ppar, abf.ap("wuslab", ppar * 2 * D, nits * D),
                    ag_u[:, it * 128:(it + nits) * 128].rearrange(
                        "(n p) m -> p n m", p=128), waits=ccw)
            slab_val = slab_vals[itp]
            nits_p = min(2, NIT - itp * 2)
            mw = [slab_val] + x1T_th[th]
            if th == 0 and it == 0:
                mw += [("dve", B.val("dve")), ("act", B.val("act"))]
            if th == 1 and it < 2:
                mw += [("dve", th_barrier_dve), ("act", th_barrier_act)]
            if (th, it - 2) in silu_done:
                mw += [("act", silu_done[(th, it - 2)]),
                       ("dve", hmul_done[(th, it - 2)])]
            def _slab(name, dm):
                return abf.ap(name, ppar * 2 * D + dm * nits_p * 128
                              + (it % 2) * 128, 128)
            for dm in range(NDM):
                mm(pb[par][:, :], _slab("wgslab", dm),
                   abf.ap("x1T", dm * TOK + th * CH, CH),
                   start=(dm == 0), stop=(dm == NDM - 1),
                   waits=mw if dm == 0 else ())
            for dm in range(NDM):
                iv = mm(pb[2 + par][:, :], _slab("wuslab", dm),
                        abf.ap("x1T", dm * TOK + th * CH, CH),
                        start=(dm == 0), stop=(dm == NDM - 1),
                        inc=("pe",) if dm == NDM - 1 else ())
            gmv = iv["pe"]
            gslab_pe[(th, it)] = gmv
            sw = [("pe", gmv)]
            if (th, it - 2) in hmul_done:
                sw.append(("dve", hmul_done[(th, it - 2)]))
            elif th == 1 and it < 2:
                sw.append(("dve", th_barrier_dve))
            silu_done[(th, it)] = act(af.ap("sg", par * 512, 512), pb[par][:, :],
                                      AF.Silu, 0.0, 1.0, waits=sw)["act"]
            hmul_done[(th, it)] = B.emit(
                "dve", lambda e, o=abf.ap("hT", it * CH, CH),
                i0=af.ap("sg", par * 512, 512), i1=pb[2 + par][:, :]:
                e.tensor_tensor(out=o, in0=i0, in1=i1, op=OP.mult),
                waits=[("act", silu_done[(th, it)]), dve_chain()],
                inc=("dve",))["dve"]
        hT_ready = max(hmul_done.values())
        last_silu = max(silu_done.values())

        dpair_pe = {}
        evd = {}
        for hcr in range(2):
            for it in range(NIT):
                par = it % 2
                w = []
                if (hcr, it - 2) in dpair_pe:
                    w.append(("pe", dpair_pe[(hcr, it - 2)]))
                elif it < 2 and prev_down_pe:
                    w.append(("pe", prev_down_pe))
                elif it < 2:
                    w.append(("pe", oproj_pe_done))
                ccw = ([("cc", cc_vals["d"])]
                       if (th == 0 and hcr == 0 and it < 2) else [])
                dfun = d_sp if par == 0 else d_gp
                sval = dfun("s_wd%d" % par, abf.ap("wdslab", par * D, 2 * CH),
                            ag_d[it * 128:(it + 1) * 128,
                                 hcr * 1024:(hcr + 1) * 1024],
                            waits=w + ccw)
                mw = [sval, ("dve", hT_ready)]
                if it == 0:
                    mw += [("dve", B.val("dve")), ("act", B.val("act"))]
                for ts2 in range(4):
                    for hc2 in range(2):
                        iv = mm(pb[ts2 * 2 + hc2][:, :],
                                abf.ap("hT", it * CH + ts2 * 128, 128),
                                abf.ap("wdslab", par * D + hc2 * 512, 512),
                                start=(it == 0), stop=(it == NIT - 1),
                                waits=mw if (ts2 == 0 and hc2 == 0) else (),
                                inc=("pe",) if (ts2 == 3 and hc2 == 1) else ())
                dpair_pe[(hcr, it)] = iv["pe"]
            down_done = dpair_pe[(hcr, NIT - 1)]
            prev_down_pe = down_done
            for ts2 in range(4):
                ts = th * 4 + ts2
                for hc2 in range(2):
                    ev = tt(af.ap("r2", ts2 * D + hcr * 1024 + hc2 * 512, 512),
                            pb[ts2 * 2 + hc2][:, :],
                            abf.ap("x1bf", ts * D + hcr * 1024 + hc2 * 512, 512),
                            OP.add,
                            waits=[("pe", down_done), dve_chain(),
                                   ("act", r2_guard.get(ts2, 0))])
                    evd[(hcr, ts2)] = ev["dve"]
        for ts2 in range(4):
            ts = th * 4 + ts2
            pr = ts2 % 2
            dvv = B.val("dve")
            for c4 in range(4):
                B.emit("dve", lambda e, o=af.ap("bn6", c4 * 6, 6),
                       i=af.ap("r2", ts2 * D + c4 * 512, 512): e.bn_stats(o, i),
                       waits=[("dve", dvv)] if c4 == 0 else (), inc=("dve",))
            B.emit("dve", lambda e, o=af.ap("ln", 0, 2), i=af.ap("bn6"):
                   e.bn_aggr(o, i), waits=[dve_chain()], inc=("dve",))
            sq = act(af.ap("ln", 2, 1), af.ap("ln", 1, 1), AF.Sqrt,
                     af.ap("epsv"), 1.0, waits=[dve_chain()])["act"]
            B.emit("dve", lambda e, o=af.ap("ln", 3, 1), i=af.ap("ln", 2, 1):
                   e.reciprocal(o, i), waits=[("act", sq), dve_chain()],
                   inc=("dve",))
            nm = B.emit("dve", lambda e, o=af.ap("ln", 4, 1),
                        i=af.ap("ln", 0, 1), s1=af.ap("ln", 3, 1):
                        e.tensor_scalar(out=o, in0=i, scalar1=s1, scalar2=-1.0,
                                        op0=OP.mult, op1=OP.mult),
                        waits=[dve_chain()], inc=("dve",))["dve"]
            ow = [("dve", nm)]
            if out_stores:
                ow.append(out_stores[-1])
            oa = act(af.ap("outb", 0, D), af.ap("r2", ts2 * D, D),
                     AF.Identity, af.ap("ln", 4, 1), af.ap("ln", 3, 1),
                     waits=ow)["act"]
            r2_guard[ts2] = oa
            sv2 = d_sp("s_out0", out_d[ts * 128:(ts + 1) * 128, :],
                       af.ap("outb", 0, D), waits=[("act", oa)])
            out_stores.append(sv2)
            th_barrier_dve = B.val("dve")
            th_barrier_act = B.val("act")

    finals = [(s, B.counts[s]) for s in list(B.counts)
              if s.startswith("s_") and B.counts[s] > 0]
    B.emit("sp", lambda e: e.nop(), waits=finals)
    B.emit("gp", lambda e: e.nop(), waits=finals)

    B.build_block()
    return nc, B



def _host_prep(inputs):
    x = np.ascontiguousarray(inputs["L0_attn_x"][0])
    Wq, bq = inputs["L0_attn_Wq"], inputs["L0_attn_bq"]
    Wk, bk = inputs["L0_attn_Wk"], inputs["L0_attn_bk"]
    Wv, bv = inputs["L0_attn_Wv"], inputs["L0_attn_bv"]
    Wo, bo = inputs["L0_attn_Wo"], inputs["L0_attn_bo"]
    lut = inputs["L0_ffn_lut"]
    g_idx = inputs["L0_ffn_gate_idx"]
    u_idx = inputs["L0_ffn_up_idx"]
    d_idx = inputs["L0_ffn_down_idx"]

    xT = np.ascontiguousarray(x.T)

    freqs = (10000.0 ** (-(np.arange(0, D, 2, dtype=np.float32) / D)))[:HD // 2]
    ang = np.outer(np.arange(T, dtype=np.float32), freqs).astype(np.float32)
    cosf = np.cos(ang).astype(np.float32).T
    sinf = np.sin(ang).astype(np.float32).T
    csa = np.empty((64, 2 * T), np.float32)
    csb = np.empty((64, 2 * T), np.float32)
    for tc in range(NCH):
        csa[:, tc * 2 * CH:tc * 2 * CH + CH] = cosf[:, tc * CH:(tc + 1) * CH]
        csa[:, tc * 2 * CH + CH:(tc + 1) * 2 * CH] = sinf[:, tc * CH:(tc + 1) * CH]
        csb[:, tc * 2 * CH:tc * 2 * CH + CH] = sinf[:, tc * CH:(tc + 1) * CH]
        csb[:, tc * 2 * CH + CH:(tc + 1) * 2 * CH] = cosf[:, tc * CH:(tc + 1) * CH]
    cosd = np.ascontiguousarray(np.concatenate([csa, csa], axis=0))
    sind = np.ascontiguousarray(np.concatenate([csb, csb], axis=0))

    kk = np.arange(128)[:, None]
    qq = np.arange(CH)[None, :]
    masks = np.concatenate(
        [((128 * j + kk) <= qq).astype(np.float32) for j in range(4)],
        axis=1).astype(bf16)

    perm = np.concatenate([np.arange(0, HD, 2), np.arange(1, HD, 2)])
    lutb = np.ascontiguousarray(np.tile(lut[None, :], (128, 1)).astype(np.float32))

    g_idxT = np.ascontiguousarray(g_idx.T)
    u_idxT = np.ascontiguousarray(u_idx.T)
    d_idxT = np.ascontiguousarray(d_idx.T)

    in_maps = []
    for c in range(NCORE):
        qh = [2 * c, 2 * c + 1]
        kvh = c // 2
        wq_c = np.empty((D, NH * HD), np.float32)
        bq_c = np.empty((NH * HD,), np.float32)
        for i, h in enumerate(qh):
            wq_c[:, i * HD:(i + 1) * HD] = Wq[:, h * HD + perm]
            bq_c[i * HD:(i + 1) * HD] = bq[h * HD + perm]
        bq_c = np.ascontiguousarray(bq_c.reshape(NH, HD).T)
        wk_c = np.ascontiguousarray(Wk[:, kvh * HD + perm])
        bk_c = np.ascontiguousarray(bk[kvh * HD + perm][:, None])
        wv_c = np.ascontiguousarray(Wv[:, kvh * HD:(kvh + 1) * HD])
        bv_c = np.ascontiguousarray(bv[kvh * HD:(kvh + 1) * HD][:, None])
        x_own_c = np.ascontiguousarray(
            np.concatenate([x[c * CH:(c + 1) * CH],
                            x[(8 + c) * CH:(9 + c) * CH]], axis=0))
        rg = slice(c * (D // NCORE), (c + 1) * (D // NCORE))
        rd = slice(c * (INTER // NCORE), (c + 1) * (INTER // NCORE))
        in_maps.append({
            "xT": xT, "x_own": x_own_c,
            "wq": np.ascontiguousarray(wq_c), "bq": bq_c,
            "wk": wk_c, "bk": bk_c, "wv": wv_c, "bv": bv_c,
            "wo": Wo, "bo": np.ascontiguousarray(bo[None, :]),
            "cosd": cosd, "sind": sind, "maskd": masks, "lutd": lutb,
            "gidx": np.ascontiguousarray(g_idxT[rg]).astype(bf16),
            "uidx": np.ascontiguousarray(u_idxT[rg]).astype(bf16),
            "didx": np.ascontiguousarray(d_idxT[rd]).astype(bf16),
        })
    return in_maps


def kernel(**inputs):
    in_maps = _host_prep(inputs)
    nc, _ = build_nc()
    res = run_bass_kernel_spmd(nc, in_maps, core_ids=list(range(NCORE)))
    out = np.empty((T, D), np.float32)
    for c in range(NCORE):
        oc = res.results[c]["out"]
        out[c * CH:(c + 1) * CH] = oc[:CH]
        out[(8 + c) * CH:(9 + c) * CH] = oc[CH:]
    return out[None, :, :]




# version 22 (final)
# speedup vs baseline: 1.1629x; 1.1629x over previous
import numpy as np
import ml_dtypes
import contextlib

import concourse.bass as bass
import concourse.mybir as mybir
from concourse.bass_utils import run_bass_kernel_spmd
from concourse.alu_op_type import AluOpType
from concourse.aot_env import allow_ici_all_to_all

F32 = mybir.dt.float32
BF16 = mybir.dt.bfloat16
AF = mybir.ActivationFunctionType
OP = AluOpType
bf16 = ml_dtypes.bfloat16

NCORE = 8
T = 8192
D = 2048
HD = 128
NH = 2
CH = 512
NCH = T // CH
INTER = 5504
NIT = INTER // 128
NDM = D // 128
TOK = 2 * CH
NTS = TOK // 128
EPS = 1e-5
OSCALE = float(D) ** -0.5
C_SHIFT = 0.0

ENGS = ("pe", "act", "dve", "gp", "sp")


class Builder:
    def __init__(self, nc):
        self.nc = nc
        self.ops = {e: [] for e in ENGS}
        self.sems = {}
        self.counts = {}
        self.es = contextlib.ExitStack()

    def sem(self, name):
        if name not in self.sems:
            self.sems[name] = self.es.enter_context(self.nc.semaphore(name))
            self.counts[name] = 0
        return self.sems[name]

    def val(self, name):
        return self.counts.get(name, 0)

    def emit(self, eng, fn, waits=(), inc=(), dma=False):
        step = 16 if dma else 1
        out_vals = {}
        incs = []
        for s in inc:
            self.sem(s)
            self.counts[s] += step
            out_vals[s] = self.counts[s]
            incs.append((self.sems[s], step))
        merged = {}
        for (s, v) in waits:
            if v > 0:
                merged[s] = max(merged.get(s, 0), v)
        wl = [(self.sem(s), v) for s, v in merged.items()]
        self.ops[eng].append((wl, fn, incs))
        return out_vals

    def build_block(self):
        ops = self.ops
        with self.nc.Block() as block:
            def runner(eng_name):
                def _run(eng):
                    for wl, fn, incs in ops[eng_name]:
                        for sem, v in wl:
                            eng.wait_ge(sem, v)
                        instr = fn(eng)
                        for sem, step in incs:
                            if instr is None:
                                instr = eng.nop()
                            instr = instr.then_inc(sem, step)
                return _run
            block.tensor(runner("pe"))
            block.scalar(runner("act"))
            block.vector(runner("dve"))
            block.gpsimd(runner("gp"))
            block.sync(runner("sp"))


class Arena:

    def __init__(self, tensor, ncols, align=16):
        self.t = tensor
        self.n = ncols
        self.align = align
        self.live = {}

    def alloc(self, name, cols):
        assert name not in self.live
        al = self.align
        taken = sorted(self.live.values())
        start = 0
        for s, ln in taken:
            if start + cols <= s:
                break
            start = -(-(s + ln) // al) * al
        assert start + cols <= self.n, (
            f"arena overflow: {name} needs {cols} @ {start}, cap {self.n}")
        self.live[name] = (start, cols)
        return start

    def free(self, *names):
        for nm in names:
            del self.live[nm]

    def ap(self, name, c0=0, cols=None, p0=0, pn=128):
        s, ln = self.live[name]
        if cols is None:
            cols = ln - c0
        assert c0 + cols <= ln, (name, c0, cols, ln)
        return self.t[p0:p0 + pn, s + c0:s + c0 + cols]


def build_nc():
    nc = bass.Bass("TRN2", target_bir_lowering=False)

    def din(name, shape, dt=F32):
        return nc.dram_tensor(name, shape, dt, kind="ExternalInput").ap()

    xT = din("xT", [D, T])
    x_own = din("x_own", [TOK, D])
    wq = din("wq", [D, NH * HD])
    wk = din("wk", [D, HD])
    wv = din("wv", [D, HD])
    wo = din("wo", [D, D])
    bq = din("bq", [128, NH])
    bk = din("bk", [HD, 1])
    bv = din("bv", [HD, 1])
    bo = din("bo", [1, D])
    cosd = din("cosd", [128, 2 * T])
    sind = din("sind", [128, 2 * T])
    maskd = din("maskd", [128, 4 * CH], BF16)
    lutd = din("lutd", [128, 16])
    gidx = din("gidx", [D // NCORE, INTER], BF16)
    uidx = din("uidx", [D // NCORE, INTER], BF16)
    didx = din("didx", [INTER // NCORE, D], BF16)
    out_d = nc.dram_tensor("out", [TOK, D], F32, kind="ExternalOutput").ap()

    ag_g_in = nc.dram_tensor("ag_g_in", [D // NCORE, INTER], BF16).ap()
    ag_u_in = nc.dram_tensor("ag_u_in", [D // NCORE, INTER], BF16).ap()
    ag_d_in = nc.dram_tensor("ag_d_in", [INTER // NCORE, D], BF16).ap()
    ag_g = nc.dram_tensor("ag_g", [D, INTER], BF16, addr_space="Shared").ap()
    ag_u = nc.dram_tensor("ag_u", [D, INTER], BF16, addr_space="Shared").ap()
    ag_d = nc.dram_tensor("ag_d", [INTER, D], BF16, addr_space="Shared").ap()
    a2a_in = [nc.dram_tensor(f"a2a_in{i}", [NCORE * NH * HD, CH], BF16).ap()
              for i in range(2)]
    a2a_out = [nc.dram_tensor(f"a2a_out{i}", [NCORE * NH * HD, CH], BF16).ap()
               for i in range(2)]
    wo_bf = nc.dram_tensor("wo_bf", [D, D], BF16).ap()

    B = Builder(nc)
    es = B.es

    BF_COLS = 80400
    F32_COLS = 13000
    abf = Arena(es.enter_context(nc.sbuf_tensor("abf", [128, BF_COLS], BF16)),
                BF_COLS, align=16)
    af = Arena(es.enter_context(nc.sbuf_tensor("af32", [128, F32_COLS], F32)),
               F32_COLS, align=8)
    pb = [es.enter_context(nc.psum_tensor(f"pb{i}", [128, 512], F32))
          for i in range(8)]

    af.alloc("lut", 16)
    af.alloc("bq", NH)
    af.alloc("bk", 1)
    af.alloc("bv", 1)
    af.alloc("bo_bc", D)
    af.alloc("sonesf", 128)
    af.alloc("ln", 16)
    af.alloc("epsv", 1)
    af.alloc("bn6", 24)
    abf.alloc("masks", 4 * CH)
    abf.alloc("sones", 1)

    def d_sp(sem, dst, src, waits=()):
        v = B.emit("sp", lambda e, dst=dst, src=src: e.dma_start(out=dst, in_=src),
                   waits=waits, inc=(sem,), dma=True)
        return (sem, v[sem])

    def d_gp(sem, dst, src, waits=()):
        v = B.emit("gp", lambda e, dst=dst, src=src: e.dma_start(out=dst, in_=src),
                   waits=waits, inc=(sem,), dma=True)
        return (sem, v[sem])

    def tsp(sem, dst, src, waits=()):
        v = B.emit("sp", lambda e, dst=dst, src=src: e.dma_start(
            out=dst, in_=src, transpose=True), waits=waits, inc=(sem,), dma=True)
        return (sem, v[sem])

    def tgp(sem, dst, src, waits=()):
        v = B.emit("act", lambda e, dst=dst, src=src: e.dma_start(
            out=dst, in_=src, transpose=True), waits=waits, inc=(sem,), dma=True)
        return (sem, v[sem])

    def mm(out, lhsT, rhs, start, stop, waits=(), inc=()):
        return B.emit("pe", lambda e, out=out, lhsT=lhsT, rhs=rhs, start=start,
                      stop=stop: e.matmul(out, lhsT, rhs, start=start, stop=stop),
                      waits=waits, inc=inc)

    def act(outp, inp, func, bias, scale, waits=()):
        return B.emit("act", lambda e, outp=outp, inp=inp, func=func, bias=bias,
                      scale=scale: e.activation(outp, inp, func, bias=bias,
                                                scale=scale),
                      waits=waits, inc=("act",))

    def tt(outp, in0, in1, op, waits=()):
        return B.emit("dve", lambda e, outp=outp, in0=in0, in1=in1, op=op:
                      e.tensor_tensor(out=outp, in0=in0, in1=in1, op=op),
                      waits=waits, inc=("dve",))

    def dve_chain():
        return ("dve", B.val("dve"))

    d_sp("s_ld", af.ap("lut"), lutd)
    d_sp("s_ld", af.ap("bq"), bq)
    d_sp("s_ld", af.ap("bk"), bk)
    d_sp("s_ld", af.ap("bv"), bv)
    d_sp("s_ld", abf.ap("masks"), maskd)
    ld0 = d_sp("s_ld", af.ap("bo_bc", 0, D, pn=1), bo)

    abf.alloc("xbuf", 2 * NDM * CH)
    abf.alloc("wq", NDM * NH * HD)
    abf.alloc("wk", NDM * HD)
    abf.alloc("wv", NDM * HD)
    af.alloc("csa", 2 * 2 * CH)
    af.alloc("csb", 2 * 2 * CH)
    af.alloc("tmpA", 2 * 2 * CH)
    af.alloc("tmpB", 2 * 2 * CH)
    af.alloc("ru", 2 * 2 * CH)
    abf.alloc("vbfT", 2 * CH)
    abf.alloc("kT", T)
    abf.alloc("qT", NH * T)
    abf.alloc("v", T)
    abf.alloc("p", 4 * CH)
    abf.alloc("o", 2 * CH)

    d_gp("s_ldw", abf.ap("wq"), wq.rearrange("(n p) m -> p n m", p=128))
    d_gp("s_ldw", abf.ap("wk"), wk.rearrange("(n p) m -> p n m", p=128))
    ldw = d_gp("s_ldw", abf.ap("wv"), wv.rearrange("(n p) m -> p n m", p=128))
    B.emit("dve", lambda e, a=abf.ap("sones"): e.memset(a, 1.0), inc=("dve",))
    B.emit("dve", lambda e, a=af.ap("epsv"): e.memset(a, EPS), inc=("dve",))
    ones_init = B.emit("dve", lambda e, a=af.ap("sonesf", pn=1): e.memset(a, OSCALE),
                       inc=("dve",))["dve"]

    def wtiles(nrows):
        tl, r = [], 0
        while r < nrows:
            tl.append((r, min(128, nrows - r)))
            r += 128
        return tl

    wb_done = {}
    gather_tiles = []
    for mat, idx_d, ag_in_d, fd in (("g", gidx, ag_g_in, INTER),
                                    ("u", uidx, ag_u_in, INTER),
                                    ("d", didx, ag_d_in, D)):
        tl = wtiles(idx_d.shape[0])
        for ti, (r0, nr) in enumerate(tl):
            gather_tiles.append((mat, idx_d, ag_in_d, fd, r0, nr,
                                 ti == len(tl) - 1))

    _gstate = {"prev_store": None, "emitted": 0}

    def emit_gather_tile(k):
        mat, idx_d, ag_in_d, fd, r0, nr, last = gather_tiles[k]
        idx_ap = abf.ap("idx", 0, fd, pn=nr)
        acc_ap = abf.ap("wacc", 0, fd, pn=nr)
        con_ap = abf.ap("wcon", 0, fd, pn=nr)
        ps = _gstate["prev_store"]
        gw = ([ps] if ps else
              [("pe", pe_xfree[NCH - 1]),
               ("act", act_evac[("q1", NCH - 1)]),
               ("dve", proj_done_dve)] + vt_final)
        lv = d_gp("s_idx", idx_ap, idx_d[r0:r0 + nr, :], waits=gw)
        B.emit("dve", lambda e, o=acc_ap, i=idx_ap, s2=af.ap("lut", 0, 1, pn=nr):
               e.tensor_scalar(out=o, in0=i, scalar1=0.0, scalar2=s2,
                               op0=OP.is_equal, op1=OP.mult),
               waits=[lv, ld0, dve_chain()], inc=("dve",))
        for j in range(1, 16):
            B.emit("dve", lambda e, o=con_ap, i=idx_ap, j=j,
                   s2=af.ap("lut", j, 1, pn=nr):
                   e.tensor_scalar(out=o, in0=i, scalar1=float(j), scalar2=s2,
                                   op0=OP.is_equal, op1=OP.mult),
                   waits=[dve_chain()], inc=("dve",))
            tt(acc_ap, acc_ap, con_ap, OP.add, waits=[dve_chain()])
        _gstate["prev_store"] = d_gp("s_wst_" + mat, ag_in_d[r0:r0 + nr, :],
                                     acc_ap, waits=[dve_chain()])
        if last:
            wb_done[mat] = _gstate["prev_store"]

    x_dma = {}
    pe_xfree = {}
    act_evac = {}
    kq_done = {}
    vT_done = {}

    for tc in range(NCH):
        par = tc % 2
        w = [("pe", pe_xfree[tc - 2])] if tc >= 2 else []
        x_dma[tc] = d_gp("s_x%d" % par, abf.ap("xbuf", par * NDM * CH, NDM * CH),
                         xT[:, tc * CH:(tc + 1) * CH].rearrange(
                             "(n p) m -> p n m", p=128), waits=w)
        csw = [("dve", kq_done[tc - 2])] if tc >= 2 else []
        d_sp("s_cs%d" % par, af.ap("csa", par * 2 * CH, 2 * CH),
             cosd[:, tc * 2 * CH:(tc + 1) * 2 * CH], waits=csw)
        cs_val = d_sp("s_cs%d" % par, af.ap("csb", par * 2 * CH, 2 * CH),
                      sind[:, tc * 2 * CH:(tc + 1) * 2 * CH])

        pew = [x_dma[tc], ldw]
        if tc >= 2:
            pew += [("act", act_evac[("k", tc - 2)]),
                    ("act", act_evac[("v", tc - 2)])]
        if tc >= 1:
            pew.append(("act", act_evac[("q1", tc - 1)]))
        for dm in range(NDM):
            iv = mm(pb[par][:, :], abf.ap("wk", dm * HD, HD),
                    abf.ap("xbuf", par * NDM * CH + dm * CH, CH),
                    start=(dm == 0), stop=(dm == NDM - 1),
                    waits=pew if dm == 0 else (),
                    inc=("pe",) if dm == NDM - 1 else ())
        pe_k = iv["pe"]
        for dm in range(NDM):
            iv = mm(pb[2 + par][:, :], abf.ap("wv", dm * HD, HD),
                    abf.ap("xbuf", par * NDM * CH + dm * CH, CH),
                    start=(dm == 0), stop=(dm == NDM - 1),
                    inc=("pe",) if dm == NDM - 1 else ())
        pe_v = iv["pe"]
        pe_q = {}
        for h in range(NH):
            for dm in range(NDM):
                iv = mm(pb[4 + h][:, :], abf.ap("wq", dm * NH * HD + h * HD, HD),
                        abf.ap("xbuf", par * NDM * CH + dm * CH, CH),
                        start=(dm == 0), stop=(dm == NDM - 1),
                        inc=("pe",) if dm == NDM - 1 else ())
            pe_q[h] = iv["pe"]
        pe_xfree[tc] = pe_q[NH - 1]

        roped_prev = kq_done.get(tc - 2, 0)
        tA = lambda c0, cols, p0=0, pn=128: af.ap("tmpA", par * 2 * CH + c0,
                                                  cols, p0=p0, pn=pn)
        tB = lambda c0, cols: af.ap("tmpB", par * 2 * CH + c0, cols, pn=64)
        act(tA(0, CH, 0, 64), pb[par][0:64, :], AF.Identity,
            af.ap("bk", p0=0, pn=64), 1.0,
            waits=[("pe", pe_k), ("dve", roped_prev), ld0])
        act(tA(CH, CH, 0, 64), pb[par][64:128, :], AF.Identity,
            af.ap("bk", p0=64, pn=64), 1.0)
        act(tA(0, CH, 64, 64), pb[4][0:64, :], AF.Identity,
            af.ap("bq", 0, 1, p0=0, pn=64), 1.0,
            waits=[("pe", pe_q[0])])
        act(tA(CH, CH, 64, 64), pb[4][64:128, :], AF.Identity,
            af.ap("bq", 0, 1, p0=64, pn=64), 1.0)
        act(tB(0, CH), pb[5][0:64, :], AF.Identity,
            af.ap("bq", 1, 1, p0=0, pn=64), 1.0,
            waits=[("pe", pe_q[1])])
        evac_last = act(tB(CH, CH), pb[5][64:128, :], AF.Identity,
                        af.ap("bq", 1, 1, p0=64, pn=64), 1.0)["act"]
        act_evac[("k", tc)] = evac_last
        act_evac[("q1", tc)] = evac_last
        vw = [("pe", pe_v)]
        if tc >= 2:
            vw.append(vT_done[tc - 2])
        act_evac[("v", tc)] = act(abf.ap("vbfT", par * CH, CH), pb[2 + par][:, :],
                                  AF.Identity, af.ap("bv"), 1.0, waits=vw)["act"]

        for j in range(CH // 128):
            vts = tsp("s_vt%d" % par, abf.ap("v", tc * CH + j * 128, 128),
                      abf.ap("vbfT", par * CH + j * 128, 128),
                      waits=[("act", act_evac[("v", tc)])] if j == 0 else ())
        vT_done[tc] = vts

        u1 = lambda p0=0, pn=128, c0=0, cols=2 * CH: af.ap("ru", c0, cols,
                                                           p0=p0, pn=pn)
        u2 = lambda p0=0, pn=128, c0=0, cols=2 * CH: af.ap("ru", 2 * CH + c0,
                                                           cols, p0=p0, pn=pn)
        csa_t = lambda pn=128: af.ap("csa", par * 2 * CH, 2 * CH, pn=pn)
        csb_t = lambda pn=128: af.ap("csb", par * 2 * CH, 2 * CH, pn=pn)
        tt(u1(), tA(0, 2 * CH), csa_t(), OP.mult,
           waits=[("act", evac_last), cs_val, dve_chain()])
        tt(u2(), tA(0, 2 * CH), csb_t(), OP.mult, waits=[dve_chain()])
        tt(abf.ap("kT", tc * CH, CH, p0=0, pn=64),
           u1(0, 64, 0, CH), u1(0, 64, CH, CH), OP.subtract,
           waits=[dve_chain()])
        tt(abf.ap("kT", tc * CH, CH, p0=64, pn=64),
           u2(0, 64, 0, CH), u2(0, 64, CH, CH), OP.add, waits=[dve_chain()])
        tt(abf.ap("qT", 0 * T + tc * CH, CH, p0=0, pn=64),
           u1(64, 64, 0, CH), u1(64, 64, CH, CH), OP.subtract,
           waits=[dve_chain()])
        tt(abf.ap("qT", 0 * T + tc * CH, CH, p0=64, pn=64),
           u2(64, 64, 0, CH), u2(64, 64, CH, CH), OP.add, waits=[dve_chain()])
        tt(u1(0, 64), tB(0, 2 * CH), csa_t(64), OP.mult, waits=[dve_chain()])
        tt(u2(0, 64), tB(0, 2 * CH), csb_t(64), OP.mult, waits=[dve_chain()])
        tt(abf.ap("qT", 1 * T + tc * CH, CH, p0=0, pn=64),
           u1(0, 64, 0, CH), u1(0, 64, CH, CH), OP.subtract,
           waits=[dve_chain()])
        tt(abf.ap("qT", 1 * T + tc * CH, CH, p0=64, pn=64),
           u2(0, 64, 0, CH), u2(0, 64, CH, CH), OP.add, waits=[dve_chain()])
        kq_done[tc] = B.val("dve")

    proj_done_dve = B.val("dve")
    vt_final = [("s_vt0", B.val("s_vt0")), ("s_vt1", B.val("s_vt1"))]

    abf.free("xbuf", "wq", "wk", "wv", "vbfT")
    af.free("csa", "csb", "tmpA", "tmpB", "ru")
    abf.alloc("idx", INTER)
    abf.alloc("wacc", INTER)
    abf.alloc("wcon", INTER)
    af.alloc("lrow", CH)
    af.alloc("lrec", CH)
    af.alloc("bc", 2 * CH)

    bo_w = [("pe", pe_xfree[NCH - 1]), ("act", act_evac[("q1", NCH - 1)]),
            ("dve", ones_init), ld0]
    bo_bc_done = 0
    for c4 in range(4):
        pv = mm(pb[4][:, :], af.ap("sonesf", 0, 128, pn=1),
                af.ap("bo_bc", c4 * 512, 512, pn=1), True, True,
                waits=bo_w if c4 == 0 else [("dve", bo_bc_done)],
                inc=("pe",))["pe"]
        bo_bc_done = B.emit("dve", lambda e, o=af.ap("bo_bc", c4 * 512, 512),
                            i=pb[c4 * 0 + 4][:, :]:
                            e.tensor_scalar(out=o, in0=i, scalar1=float(D) ** 0.5,
                                            scalar2=None, op0=OP.mult,
                                            op1=OP.bypass),
                            waits=[("pe", pv), dve_chain()], inc=("dve",))["dve"]

    NKT = CH // 128
    o_done = {}
    pslot_free = {}
    bank_free_qk = {}
    pv_free = {}
    bc_free = {}
    l_free = 0
    bv_prev = 0
    bc_prev = 0
    o_store_seq = {}
    attn_barrier = [("dve", max(proj_done_dve, bo_bc_done)), ld0] + vt_final
    gg = 0
    seq = 0
    bank_free = {}
    slot_free = {}
    pending_norm = [None]

    def emit_norm(hp, seq_, qc_, h_, lv_):
        B.emit("dve", lambda e, o=af.ap("lrow", 0, CH, pn=1),
               i=pb[5][0:1, :]: e.tensor_copy(o, i),
               waits=[("pe", lv_), dve_chain()], inc=("dve",))
        lf = B.emit("dve", lambda e, o=af.ap("lrec", 0, CH, pn=1),
                    i=af.ap("lrow", 0, CH, pn=1): e.reciprocal(o, i),
                    waits=[dve_chain(), ("pe", _norm["bv_prev"])],
                    inc=("dve",))["dve"]
        _norm["l_free"] = lf
        bw = [("dve", lf), ("dve", _norm["bc_prev"])]
        bv_ = mm(pb[4][:, :], af.ap("sonesf", 0, 128, pn=1),
                 af.ap("lrec", 0, CH, pn=1), True, True, waits=bw,
                 inc=("pe",))["pe"]
        _norm["bv_prev"] = bv_
        cb = B.emit("dve", lambda e, o=af.ap("bc", hp * CH, CH),
                    i=pb[4][:, :]: e.tensor_copy(o, i),
                    waits=[("pe", bv_), dve_chain()], inc=("dve",))["dve"]
        _norm["bc_prev"] = cb
        ow = [("dve", cb)]
        if seq_ >= 2:
            ow.append(o_store_seq[seq_ - 2])
        om = tt(abf.ap("o", hp * CH, CH), pb[6 + hp][:, :],
                af.ap("bc", hp * CH, CH), OP.mult, waits=ow)["dve"]
        pv_free[hp] = om
        half = qc_ // (NCH // 2)
        dest = qc_ % (NCH // 2)
        ov = d_sp("s_o%d%d" % (half, hp),
                  a2a_in[half][dest * NH * HD + h_ * HD:
                               dest * NH * HD + (h_ + 1) * HD, :],
                  abf.ap("o", hp * CH, CH), waits=[("dve", om)])
        o_done[(h_, qc_)] = ov
        o_store_seq[seq_] = ov
        ngt = len(gather_tiles)
        want = min(ngt, (seq_ + 1) * ngt // 24)
        while _gstate["emitted"] < want:
            emit_gather_tile(_gstate["emitted"])
            _gstate["emitted"] += 1

    _norm = {"bv_prev": 0, "bc_prev": 0, "l_free": 0}
    for qc in range(NCH):
        for h in range(NH):
            hp = seq % 2
            ng2 = 2 * (qc + 1)
            nkt = 4 * (qc + 1)
            pend_avl = None
            lv = None
            for g in range(ng2 + 1):
                if g < ng2:
                    bp = gg % 2
                    w = list(attn_barrier) if gg == 0 and seq == 0 else []
                    if bp in bank_free:
                        w.append(("act", bank_free[bp]))
                    for j in range(2):
                        kt = 2 * g + j
                        qv = mm(pb[2 * bp + j][:, :],
                                abf.ap("kT", kt * 128, 128),
                                abf.ap("qT", h * T + qc * CH, CH), True, True,
                                waits=w if j == 0 else (),
                                inc=("pe",) if j == 1 else ())
                    qk_g = qv["pe"]
                    if g == 0 and pending_norm[0] is not None:
                        pending_norm[0]()
                        pending_norm[0] = None
                    ew = [("pe", qk_g)]
                    if bp in slot_free:
                        ew.append(("pe", slot_free[bp]))
                    for j in range(2):
                        avv = B.emit(
                            "act", lambda e, o=abf.ap("p", (2 * bp + j) * CH, CH),
                            i=pb[2 * bp + j][:, :]:
                            e.activation(o, i, AF.Exp, bias=-C_SHIFT, scale=1.0),
                            waits=ew if j == 0 else (), inc=("act",))["act"]
                    bank_free[bp] = avv
                    pdep = ("act", avv)
                    if 2 * g >= nkt - 4:
                        for j in range(2):
                            kt = 2 * g + j
                            mj = kt - (nkt - 4)
                            mv = tt(abf.ap("p", (2 * bp + j) * CH, CH),
                                    abf.ap("p", (2 * bp + j) * CH, CH),
                                    abf.ap("masks", mj * CH, CH), OP.mult,
                                    waits=[("act", avv), dve_chain()] if j == 0
                                    else [dve_chain()])["dve"]
                        pdep = ("dve", mv)
                if pend_avl is not None:
                    pg, ppdep = pend_avl
                    pbp = pg % 2
                    for j in range(2):
                        kt = 2 * (pg - gg0) + j
                        avw = []
                        if j == 0:
                            avw.append(ppdep)
                            if kt == 0 and hp in pv_free:
                                avw.append(("dve", pv_free[hp]))
                        mm(pb[6 + hp][:, :], abf.ap("v", kt * 128, 128),
                           abf.ap("p", (2 * pbp + j) * CH, CH),
                           start=(kt == 0), stop=(kt == nkt - 1), waits=avw)
                        lw = ([("dve", _norm["l_free"])]
                              if (kt == 0 and _norm["l_free"]) else [])
                        lvv = mm(pb[5][0:1, :], abf.ap("sones"),
                                 abf.ap("p", (2 * pbp + j) * CH, CH),
                                 start=(kt == 0), stop=(kt == nkt - 1),
                                 waits=lw, inc=("pe",) if j == 1 else ())
                        if j == 1:
                            lv = lvv["pe"]
                    slot_free[pbp] = lv
                if g < ng2:
                    if g == 0:
                        gg0 = gg
                    pend_avl = (gg, pdep)
                    gg += 1
            pending_norm[0] = (lambda hp=hp, s=seq, qc_=qc, h_=h, lv_=lv:
                               emit_norm(hp, s, qc_, h_, lv_))
            seq += 1
    pending_norm[0]()
    pending_norm[0] = None

    while _gstate["emitted"] < len(gather_tiles):
        emit_gather_tile(_gstate["emitted"])
        _gstate["emitted"] += 1

    cc_vals = {}

    def collective(kind, ins, outs, waits):
        def fn(e, kind=kind, ins=ins, outs=outs):
            if kind == "AllToAll":
                with allow_ici_all_to_all():
                    return e.collective_compute(
                        kind, OP.bypass, replica_groups=[list(range(NCORE))],
                        ins=ins, outs=outs)
            return e.collective_compute(
                kind, OP.bypass, replica_groups=[list(range(NCORE))],
                ins=ins, outs=outs)
        return B.emit("gp", fn, waits=waits, inc=("cc",))["cc"]

    cc_vals["g"] = collective("AllGather", [ag_g_in.opt()], [ag_g.opt()],
                              [wb_done["g"]])
    cc_vals["u"] = collective("AllGather", [ag_u_in.opt()], [ag_u.opt()],
                              [wb_done["u"]])
    cc_vals["a1"] = collective(
        "AllToAll", [a2a_in[0].opt()], [a2a_out[0].opt()],
        [("s_o00", B.val("s_o00")), ("s_o01", B.val("s_o01"))])
    cc_vals["a2"] = collective(
        "AllToAll", [a2a_in[1].opt()], [a2a_out[1].opt()],
        [("s_o10", B.val("s_o10")), ("s_o11", B.val("s_o11"))])
    cc_vals["d"] = collective("AllGather", [ag_d_in.opt()], [ag_d.opt()],
                              [wb_done["d"]])
    for q4 in range(4):
        wo_cast = d_gp("s_wocast", wo_bf[q4 * 512:(q4 + 1) * 512, :],
                       wo[q4 * 512:(q4 + 1) * 512, :])

    abf.free("kT", "qT", "v", "p", "o", "idx", "wacc", "wcon")
    af.free("lrow", "lrec", "bc")

    abf.alloc("oown", NDM * TOK)
    abf.alloc("woslab", 2 * D)
    abf.alloc("x1bf", NTS * D)
    abf.alloc("x1T", NDM * TOK)
    af.alloc("xbo", 2 * D)
    af.alloc("r1", 2 * D)

    gather_flushed = [wb_done["d"], ("s_idx", B.val("s_idx"))]
    xv_pre = {}
    for ts in (0, 1):
        xv_pre[ts] = d_sp("s_xo%d" % ts, af.ap("xbo", ts * D, D),
                          x_own[ts * 128:(ts + 1) * 128, :], waits=[])
    for half in range(2):
        for dt in range(NDM):
            d_sp("s_ol%d" % half, abf.ap("oown", dt * TOK + half * CH, CH),
                 a2a_out[half][dt * 128:(dt + 1) * 128, :],
                 waits=[("cc", cc_vals["a1" if half == 0 else "a2"])]
                 + (gather_flushed if (dt == 0 and half == 0) else []))
    oload_h = [("s_ol0", B.val("s_ol0")), ("s_ol1", B.val("s_ol1"))]

    NOROUND = NTS // 2
    wo_pe = {}
    evac_done = {}
    x1_done = {}
    x1T_done = {}
    xbo_add = {}

    for rnd in range(NOROUND):
        ts0, ts1 = 2 * rnd, 2 * rnd + 1
        for ts in (ts0, ts1):
            pr = ts % 2
            w = []
            if ts >= 2:
                w.append(("dve", evac_done[(rnd - 1, (ts % 2) * 4 + 3)]))
            xv = xv_pre[ts] if ts in xv_pre else d_sp(
                "s_xo%d" % pr, af.ap("xbo", pr * D, D),
                x_own[ts * 128:(ts + 1) * 128, :], waits=w)
            xbo_add[ts] = tt(af.ap("xbo", pr * D, D), af.ap("xbo", pr * D, D),
                             af.ap("bo_bc"), OP.add,
                             waits=[xv, ("dve", max(B.val("dve"),
                                                    bo_bc_done))])["dve"]
        for dt in range(NDM):
            par = dt % 2
            w = []
            key_prev = (rnd, dt - 2) if dt >= 2 else (rnd - 1, dt + NDM - 2)
            if key_prev in wo_pe:
                w.append(("pe", wo_pe[key_prev]))
            ww = w + ([wo_cast] if (rnd == 0 and dt < 2) else [])
            gval = d_gp("s_wo%d" % par, abf.ap("woslab", par * D, D),
                        wo_bf[dt * 128:(dt + 1) * 128, :], waits=ww)
            for i, ts in enumerate((ts0, ts1)):
                mw = [gval]
                if dt == 0:
                    mw.append(oload_h[0] if rnd < 2 else oload_h[1])
                    for c4 in range(4):
                        if (rnd - 1, i * 4 + c4) in evac_done:
                            mw.append(("dve", evac_done[(rnd - 1, i * 4 + c4)]))
                for c4 in range(4):
                    iv = mm(pb[i * 4 + c4][:, :],
                            abf.ap("oown", dt * TOK + ts * 128, 128),
                            abf.ap("woslab", par * D + c4 * 512, 512),
                            start=(dt == 0), stop=(dt == NDM - 1),
                            waits=mw if c4 == 0 else (),
                            inc=("pe",) if (c4 == 3 and i == 1) else ())
            wo_pe[(rnd, dt)] = iv["pe"]
        mm_done = wo_pe[(rnd, NDM - 1)]
        for i, ts in enumerate((ts0, ts1)):
            pr = ts % 2
            for c4 in range(4):
                ev = tt(af.ap("r1", pr * D + c4 * 512, 512), pb[i * 4 + c4][:, :],
                        af.ap("xbo", pr * D + c4 * 512, 512), OP.add,
                        waits=[("pe", mm_done),
                               ("act", x1_done.get(ts - 2, 0)),
                               ("dve", max(B.val("dve"), xbo_add[ts]))])["dve"]
                evac_done[(rnd, i * 4 + c4)] = ev
            dvw = B.val("dve")
            for c4 in range(4):
                B.emit("dve", lambda e, o=af.ap("bn6", c4 * 6, 6),
                       i=af.ap("r1", pr * D + c4 * 512, 512): e.bn_stats(o, i),
                       waits=[("dve", dvw)] if c4 == 0 else (), inc=("dve",))
            B.emit("dve", lambda e, o=af.ap("ln", 0, 2), i=af.ap("bn6"):
                   e.bn_aggr(o, i), waits=[dve_chain()], inc=("dve",))
            sq = act(af.ap("ln", 2, 1), af.ap("ln", 1, 1), AF.Sqrt, af.ap("epsv"), 1.0,
                     waits=[dve_chain()])["act"]
            B.emit("dve", lambda e, o=af.ap("ln", 3, 1), i=af.ap("ln", 2, 1):
                   e.reciprocal(o, i), waits=[("act", sq), dve_chain()],
                   inc=("dve",))
            nm = B.emit("dve", lambda e, o=af.ap("ln", 4, 1), i=af.ap("ln", 0, 1),
                        s1=af.ap("ln", 3, 1):
                        e.tensor_scalar(out=o, in0=i, scalar1=s1, scalar2=-1.0,
                                        op0=OP.mult, op1=OP.mult),
                        waits=[dve_chain()], inc=("dve",))["dve"]
            x1_done[ts] = act(abf.ap("x1bf", ts * D, D), af.ap("r1", pr * D, D),
                              AF.Identity, af.ap("ln", 4, 1), af.ap("ln", 3, 1),
                              waits=[("dve", nm)])["act"]
            sem_t = "s_x1t%d" % (ts // 4)
            for dt in range(NDM):
                tv = tsp(sem_t, abf.ap("x1T", dt * TOK + ts * 128, 128),
                         abf.ap("x1bf", ts * D + dt * 128, 128),
                         waits=[("act", x1_done[ts])] if dt == 0 else ())
            x1T_done[ts] = [(sem_t, B.val(sem_t))]
    x1T_all = [("s_x1t0", B.val("s_x1t0")), ("s_x1t1", B.val("s_x1t1"))]
    x1T_th = {0: x1T_done[3], 1: x1T_done[7]}
    oproj_pe_done = wo_pe[(NOROUND - 1, NDM - 1)]

    abf.free("oown", "woslab")
    af.free("xbo", "r1")
    abf.alloc("hT", NIT * CH)
    abf.alloc("wgslab", 2 * 2 * D)
    abf.alloc("wuslab", 2 * 2 * D)
    abf.alloc("wdslab", 2 * D)
    af.free("bo_bc", "sonesf")
    af.alloc("sg", 2 * 512)
    af.alloc("r2", 4 * D)
    af.alloc("outb", D)

    out_stores = []
    r2_guard = {0: 0, 1: 0}
    gslab_pe = {}
    prev_down_pe = 0
    th_barrier_dve = 0
    th_barrier_act = 0
    for th in range(2):
        silu_done = {}
        hmul_done = {}
        slab_vals = {}
        for it in range(NIT):
            par = it % 2
            itp = it // 2
            ppar = itp % 2
            if it % 2 == 0:
                nits = min(2, NIT - it)
                w = []
                if (th, it - 4) in gslab_pe:
                    w.append(("pe", gslab_pe[(th, it - 3)]))
                elif th == 1 and it < 4 and (th - 1, NIT - 4 + it) in gslab_pe:
                    w.append(("pe", gslab_pe[(th - 1, NIT - 3 + it)]))
                if th == 1 and it < 4:
                    w.append(("pe", prev_down_pe))
                ccw = [("cc", cc_vals["g"])] if (th == 0 and it < 4) else []
                d_sp("s_wgu%d" % ppar, abf.ap("wgslab", ppar * 2 * D, nits * D),
                     ag_g[:, it * 128:(it + nits) * 128].rearrange(
                         "(n p) m -> p n m", p=128), waits=w + ccw)
                ccw = [("cc", cc_vals["u"])] if (th == 0 and it < 4) else []
                slab_vals[itp] = d_sp(
                    "s_wgu%d" % ppar, abf.ap("wuslab", ppar * 2 * D, nits * D),
                    ag_u[:, it * 128:(it + nits) * 128].rearrange(
                        "(n p) m -> p n m", p=128), waits=ccw)
            slab_val = slab_vals[itp]
            nits_p = min(2, NIT - itp * 2)
            mw = [slab_val] + x1T_th[th]
            if th == 0 and it == 0:
                mw += [("dve", B.val("dve")), ("act", B.val("act"))]
            if th == 1 and it < 2:
                mw += [("dve", th_barrier_dve), ("act", th_barrier_act)]
            if (th, it - 2) in silu_done:
                mw += [("act", silu_done[(th, it - 2)]),
                       ("dve", hmul_done[(th, it - 2)])]
            def _slab(name, dm):
                return abf.ap(name, ppar * 2 * D + dm * nits_p * 128
                              + (it % 2) * 128, 128)
            for dm in range(NDM):
                mm(pb[par][:, :], _slab("wgslab", dm),
                   abf.ap("x1T", dm * TOK + th * CH, CH),
                   start=(dm == 0), stop=(dm == NDM - 1),
                   waits=mw if dm == 0 else ())
            for dm in range(NDM):
                iv = mm(pb[2 + par][:, :], _slab("wuslab", dm),
                        abf.ap("x1T", dm * TOK + th * CH, CH),
                        start=(dm == 0), stop=(dm == NDM - 1),
                        inc=("pe",) if dm == NDM - 1 else ())
            gmv = iv["pe"]
            gslab_pe[(th, it)] = gmv
            sw = [("pe", gmv)]
            if (th, it - 2) in hmul_done:
                sw.append(("dve", hmul_done[(th, it - 2)]))
            elif th == 1 and it < 2:
                sw.append(("dve", th_barrier_dve))
            silu_done[(th, it)] = act(af.ap("sg", par * 512, 512), pb[par][:, :],
                                      AF.Silu, 0.0, 1.0, waits=sw)["act"]
            hmul_done[(th, it)] = B.emit(
                "dve", lambda e, o=abf.ap("hT", it * CH, CH),
                i0=af.ap("sg", par * 512, 512), i1=pb[2 + par][:, :]:
                e.tensor_tensor(out=o, in0=i0, in1=i1, op=OP.mult),
                waits=[("act", silu_done[(th, it)]), dve_chain()],
                inc=("dve",))["dve"]
        hT_ready = max(hmul_done.values())
        last_silu = max(silu_done.values())

        dpair_pe = {}
        evd = {}
        for hcr in range(2):
            for it in range(NIT):
                par = it % 2
                w = []
                if (hcr, it - 2) in dpair_pe:
                    w.append(("pe", dpair_pe[(hcr, it - 2)]))
                elif it < 2 and prev_down_pe:
                    w.append(("pe", prev_down_pe))
                elif it < 2:
                    w.append(("pe", oproj_pe_done))
                ccw = ([("cc", cc_vals["d"])]
                       if (th == 0 and hcr == 0 and it < 2) else [])
                dfun = d_sp if par == 0 else d_gp
                sval = dfun("s_wd%d" % par, abf.ap("wdslab", par * D, 2 * CH),
                            ag_d[it * 128:(it + 1) * 128,
                                 hcr * 1024:(hcr + 1) * 1024],
                            waits=w + ccw)
                mw = [sval, ("dve", hT_ready)]
                if it == 0:
                    mw += [("dve", B.val("dve")), ("act", B.val("act"))]
                for ts2 in range(4):
                    for hc2 in range(2):
                        iv = mm(pb[ts2 * 2 + hc2][:, :],
                                abf.ap("hT", it * CH + ts2 * 128, 128),
                                abf.ap("wdslab", par * D + hc2 * 512, 512),
                                start=(it == 0), stop=(it == NIT - 1),
                                waits=mw if (ts2 == 0 and hc2 == 0) else (),
                                inc=("pe",) if (ts2 == 3 and hc2 == 1) else ())
                dpair_pe[(hcr, it)] = iv["pe"]
            down_done = dpair_pe[(hcr, NIT - 1)]
            prev_down_pe = down_done
            for ts2 in range(4):
                ts = th * 4 + ts2
                for hc2 in range(2):
                    ev = tt(af.ap("r2", ts2 * D + hcr * 1024 + hc2 * 512, 512),
                            pb[ts2 * 2 + hc2][:, :],
                            abf.ap("x1bf", ts * D + hcr * 1024 + hc2 * 512, 512),
                            OP.add,
                            waits=[("pe", down_done), dve_chain(),
                                   ("act", r2_guard.get(ts2, 0))])
                    evd[(hcr, ts2)] = ev["dve"]
        for ts2 in range(4):
            ts = th * 4 + ts2
            pr = ts2 % 2
            dvv = B.val("dve")
            for c4 in range(4):
                B.emit("dve", lambda e, o=af.ap("bn6", c4 * 6, 6),
                       i=af.ap("r2", ts2 * D + c4 * 512, 512): e.bn_stats(o, i),
                       waits=[("dve", dvv)] if c4 == 0 else (), inc=("dve",))
            B.emit("dve", lambda e, o=af.ap("ln", 0, 2), i=af.ap("bn6"):
                   e.bn_aggr(o, i), waits=[dve_chain()], inc=("dve",))
            sq = act(af.ap("ln", 2, 1), af.ap("ln", 1, 1), AF.Sqrt,
                     af.ap("epsv"), 1.0, waits=[dve_chain()])["act"]
            B.emit("dve", lambda e, o=af.ap("ln", 3, 1), i=af.ap("ln", 2, 1):
                   e.reciprocal(o, i), waits=[("act", sq), dve_chain()],
                   inc=("dve",))
            nm = B.emit("dve", lambda e, o=af.ap("ln", 4, 1),
                        i=af.ap("ln", 0, 1), s1=af.ap("ln", 3, 1):
                        e.tensor_scalar(out=o, in0=i, scalar1=s1, scalar2=-1.0,
                                        op0=OP.mult, op1=OP.mult),
                        waits=[dve_chain()], inc=("dve",))["dve"]
            ow = [("dve", nm)]
            if out_stores:
                ow.append(out_stores[-1])
            oa = act(af.ap("outb", 0, D), af.ap("r2", ts2 * D, D),
                     AF.Identity, af.ap("ln", 4, 1), af.ap("ln", 3, 1),
                     waits=ow)["act"]
            r2_guard[ts2] = oa
            sv2 = d_sp("s_out0", out_d[ts * 128:(ts + 1) * 128, :],
                       af.ap("outb", 0, D), waits=[("act", oa)])
            out_stores.append(sv2)
            th_barrier_dve = B.val("dve")
            th_barrier_act = B.val("act")

    finals = [(s, B.counts[s]) for s in list(B.counts)
              if s.startswith("s_") and B.counts[s] > 0]
    B.emit("sp", lambda e: e.nop(), waits=finals)
    B.emit("gp", lambda e: e.nop(), waits=finals)

    B.build_block()
    return nc, B



def _host_prep(inputs):
    x = np.ascontiguousarray(inputs["L0_attn_x"][0])
    Wq, bq = inputs["L0_attn_Wq"], inputs["L0_attn_bq"]
    Wk, bk = inputs["L0_attn_Wk"], inputs["L0_attn_bk"]
    Wv, bv = inputs["L0_attn_Wv"], inputs["L0_attn_bv"]
    Wo, bo = inputs["L0_attn_Wo"], inputs["L0_attn_bo"]
    lut = inputs["L0_ffn_lut"]
    g_idx = inputs["L0_ffn_gate_idx"]
    u_idx = inputs["L0_ffn_up_idx"]
    d_idx = inputs["L0_ffn_down_idx"]

    xT = np.ascontiguousarray(x.T)

    freqs = (10000.0 ** (-(np.arange(0, D, 2, dtype=np.float32) / D)))[:HD // 2]
    ang = np.outer(np.arange(T, dtype=np.float32), freqs).astype(np.float32)
    cosf = np.cos(ang).astype(np.float32).T
    sinf = np.sin(ang).astype(np.float32).T
    csa = np.empty((64, 2 * T), np.float32)
    csb = np.empty((64, 2 * T), np.float32)
    for tc in range(NCH):
        csa[:, tc * 2 * CH:tc * 2 * CH + CH] = cosf[:, tc * CH:(tc + 1) * CH]
        csa[:, tc * 2 * CH + CH:(tc + 1) * 2 * CH] = sinf[:, tc * CH:(tc + 1) * CH]
        csb[:, tc * 2 * CH:tc * 2 * CH + CH] = sinf[:, tc * CH:(tc + 1) * CH]
        csb[:, tc * 2 * CH + CH:(tc + 1) * 2 * CH] = cosf[:, tc * CH:(tc + 1) * CH]
    cosd = np.ascontiguousarray(np.concatenate([csa, csa], axis=0))
    sind = np.ascontiguousarray(np.concatenate([csb, csb], axis=0))

    kk = np.arange(128)[:, None]
    qq = np.arange(CH)[None, :]
    masks = np.concatenate(
        [((128 * j + kk) <= qq).astype(np.float32) for j in range(4)],
        axis=1).astype(bf16)

    perm = np.concatenate([np.arange(0, HD, 2), np.arange(1, HD, 2)])
    lutb = np.ascontiguousarray(np.tile(lut[None, :], (128, 1)).astype(np.float32))

    g_idxT = np.ascontiguousarray(g_idx.T)
    u_idxT = np.ascontiguousarray(u_idx.T)
    d_idxT = np.ascontiguousarray(d_idx.T)

    in_maps = []
    for c in range(NCORE):
        qh = [2 * c, 2 * c + 1]
        kvh = c // 2
        wq_c = np.empty((D, NH * HD), np.float32)
        bq_c = np.empty((NH * HD,), np.float32)
        for i, h in enumerate(qh):
            wq_c[:, i * HD:(i + 1) * HD] = Wq[:, h * HD + perm]
            bq_c[i * HD:(i + 1) * HD] = bq[h * HD + perm]
        bq_c = np.ascontiguousarray(bq_c.reshape(NH, HD).T)
        wk_c = np.ascontiguousarray(Wk[:, kvh * HD + perm])
        bk_c = np.ascontiguousarray(bk[kvh * HD + perm][:, None])
        wv_c = np.ascontiguousarray(Wv[:, kvh * HD:(kvh + 1) * HD])
        bv_c = np.ascontiguousarray(bv[kvh * HD:(kvh + 1) * HD][:, None])
        x_own_c = np.ascontiguousarray(
            np.concatenate([x[c * CH:(c + 1) * CH],
                            x[(8 + c) * CH:(9 + c) * CH]], axis=0))
        rg = slice(c * (D // NCORE), (c + 1) * (D // NCORE))
        rd = slice(c * (INTER // NCORE), (c + 1) * (INTER // NCORE))
        in_maps.append({
            "xT": xT, "x_own": x_own_c,
            "wq": np.ascontiguousarray(wq_c), "bq": bq_c,
            "wk": wk_c, "bk": bk_c, "wv": wv_c, "bv": bv_c,
            "wo": Wo, "bo": np.ascontiguousarray(bo[None, :]),
            "cosd": cosd, "sind": sind, "maskd": masks, "lutd": lutb,
            "gidx": np.ascontiguousarray(g_idxT[rg]).astype(bf16),
            "uidx": np.ascontiguousarray(u_idxT[rg]).astype(bf16),
            "didx": np.ascontiguousarray(d_idxT[rd]).astype(bf16),
        })
    return in_maps


def kernel(**inputs):
    in_maps = _host_prep(inputs)
    nc, _ = build_nc()
    res = run_bass_kernel_spmd(nc, in_maps, core_ids=list(range(NCORE)))
    out = np.empty((T, D), np.float32)
    for c in range(NCORE):
        oc = res.results[c]["out"]
        out[c * CH:(c + 1) * CH] = oc[:CH]
        out[(8 + c) * CH:(9 + c) * CH] = oc[CH:]
    return out[None, :, :]


